# version 08
# baseline (speedup 1.0000x reference)
import numpy as np
import concourse.bass as bass
import concourse.mybir as mybir
from concourse.bass_utils import run_bass_kernel_spmd

F32 = mybir.dt.float32
BF16 = mybir.dt.bfloat16
I32 = mybir.dt.int32
A = mybir.AluOpType
AF = mybir.ActivationFunctionType
AX = mybir.AxisListType.X

B, T, D = 4, 4096, 1024
NH, HD, NPAIR, DC = 16, 64, 8, 8
IFF = 2816
SCALE = 0.125
EPS = 1e-5
NEG = -1.0e30
T_OWN, T_LOC, OWN0 = 2048, 2304, 128
NT, NTL, NG, GSZ = 16, 18, 6, 384
FARG, T_FAR = 5, 1920
NKV = 5
NGW = IFF // 128
ICH = [(0, 512), (512, 512), (1024, 512), (1536, 512), (2048, 512), (2560, 256)]

STAGE = 5


def build_nc(stage=None, taps=()):
    stage = STAGE if stage is None else stage
    nc = bass.Bass("TRN2", target_bir_lowering=False, detect_race_conditions=False, dynamic_dma_scratch_size=1024)

    dt_ = nc.dram_tensor
    x_loc = dt_("x_loc", [T_LOC, D], F32, kind="ExternalInput").ap()
    x_far = dt_("x_far", [T_FAR, D], F32, kind="ExternalInput").ap()
    x0_in = dt_("x0", [1, D], F32, kind="ExternalInput").ap()
    cfm_in = dt_("c_fm", [128, 8], F32, kind="ExternalInput").ap()
    wmod_in = dt_("wmod", [D, 6 * D], F32, kind="ExternalInput").ap()
    w_in = {n: dt_(n, [D, D], F32, kind="ExternalInput").ap()
            for n in ("wq", "wk", "wv", "wo")}
    wg_in = dt_("wg", [IFF, D], F32, kind="ExternalInput").ap()
    wu_in = dt_("wu", [IFF, D], F32, kind="ExternalInput").ap()
    wd_in = dt_("wd", [D, IFF], F32, kind="ExternalInput").ap()
    masks_in = dt_("masks", [NT, 128, GSZ], BF16, kind="ExternalInput").ap()
    qg_in = dt_("qg", [128, 1], F32, kind="ExternalInput").ap()
    ident_in = dt_("ident", [128, 128], BF16, kind="ExternalInput").ap()
    e0_in = dt_("e0", [128, 128], BF16, kind="ExternalInput").ap()
    hm_in = dt_("halfmask", [128, 2], BF16, kind="ExternalInput").ap()
    pm_in = dt_("parmask", [16, 8], BF16, kind="ExternalInput").ap()
    ep_in = dt_("epar", [128, 128], BF16, kind="ExternalInput").ap()

    out_hbm = dt_("out", [T_OWN, D], F32, kind="ExternalOutput").ap()

    x2s = dt_("x2s", [T_OWN, D], BF16, kind="Internal").ap()
    h2qT_h = dt_("h2qT_h", [D, T_OWN], BF16, kind="Internal").ap()

    wgq_h = dt_("wgq_h", [D, IFF], BF16, kind="Internal").ap()
    wuq_h = dt_("wuq_h", [D, IFF], BF16, kind="Internal").ap()
    wdq_h = dt_("wdq_h", [IFF, D], BF16, kind="Internal").ap()

    dbg = {}

    def tap(name, shape, dtp=F32):
        if name in taps:
            dbg[name] = dt_("dbg_" + name, shape, dtp, kind="ExternalOutput").ap()
        return dbg.get(name)

    sb = nc.alloc_sbuf_tensor
    c_fm = sb("c_fm_s", [128, 8], F32)
    mod_fm = sb("mod_fm", [128, 48], F32)
    ca1 = sb("ca1", [128, 8], F32)
    cb1 = sb("cb1", [128, 8], F32)
    gam = sb("gam", [128, 8], F32)
    gbm = sb("gbm", [128, 8], F32)
    sab = sb("sab", [128, 8], BF16)
    bqv = sb("bqv", [128, 8], F32)
    bkv = sb("bkv", [128, 8], F32)
    bvv = sb("bvv", [128, 8], F32)
    bkb = sb("bkb", [128, 8], BF16)
    ident = sb("ident_s", [128, 128], BF16)
    e0 = sb("e0_s", [128, 128], BF16)
    halfm = sb("halfm", [128, 2], BF16)
    parmask = sb("parmask_s", [128, 8], BF16)
    epar = sb("epar_s", [128, 128], BF16)
    qg = sb("qg_s", [128, 1], F32)
    masks = sb("masks_s", [128, 2 * GSZ], BF16)
    gaB = sb("gaB", [128, D], BF16)
    gbB = sb("gbB", [128, D], BF16)
    cb1B = sb("cb1B", [128, D], BF16)
    sbB = sb("sbB", [128, D], BF16)
    ca1B = sb("ca1B", [128, D], BF16)
    bvB = sb("bvB", [128, D], BF16)
    q0f = sb("q0f", [128, 8], BF16)
    x0f = sb("x0f", [128, 8], BF16)
    k0f = sb("k0f", [128, 8], BF16)
    v0f = sb("v0f", [128, 8], BF16)
    uf = sb("uf", [128, 8], BF16)
    upk = [sb(f"upk{i}", [128, 128], BF16) for i in range(DC)]
    q0pk = [sb(f"q0pk{i}", [128, 128], BF16) for i in range(DC)]
    ufm = sb("ufm", [128, 128], BF16)
    k0pk = [sb(f"k0pk{i}", [128, 128], BF16) for i in range(DC)]
    v0hm = [sb(f"v0hm{i}", [128, 128], BF16) for i in range(DC)]
    c0col = sb("c0col", [128, 1], F32)
    padT = sb("padT", [128, 128], BF16)
    padC = sb("padC", [128, 128], BF16)


    wfm = {n: [sb(f"{n}f{i}", [128, D], BF16) for i in range(DC)]
           for n in ("wq", "wk", "wv", "wo")}
    wst = [sb(f"wst{i}", [128, D], F32) for i in range(2)]
    wbf = sb("wbf", [128, D], BF16)
    silc = sb("silc", [128, 8], BF16)

    xst = [sb(f"xst{i}", [128, D], F32) for i in range(2)]
    x0row = xst[0]
    xbf = sb("xbf", [128, 6 * D], BF16)
    mvst = sb("mvst", [128, 12], F32)
    mva = sb("mva", [128, 2], F32)
    rstd = sb("rstd", [128, 1], F32)
    vtmp = sb("vtmp", [128, 1], F32)
    nbias = sb("nbias", [128, 1], F32)
    xn_tm = sb("xn_tm", [128, 4 * D], BF16)
    xnT = sb("xnT", [128, 2 * DC * GSZ], BF16)

    Qs = [sb(f"Qs{i}", [128, NKV * 128], BF16) for i in range(DC)]
    Ks = [sb(f"Ks{i}", [128, NKV * 128], BF16) for i in range(DC)]
    Vs = [sb(f"Vs{i}", [128, D], BF16) for i in range(NKV)]
    l0buf = sb("l0buf", [128, (NTL + 1) * 16], F32)
    l0st = sb("l0st", [128, GSZ], BF16)
    qe = sb("qe", [128, 128], BF16)
    qo = sb("qo", [128, 128], BF16)
    scs = sb("scs", [128, GSZ], F32)
    pbf = sb("pbf", [128, GSZ], BF16)
    pts = sb("pts", [128, GSZ], BF16)
    p0tm = sb("p0tm", [128, 16], BF16)
    p0T = sb("p0T", [128, 128], BF16)
    mxv = sb("mxv", [128, 1], F32)
    ssum = sb("ssum", [128, 1], F32)
    rcpv = sb("rcpv", [128, 1], F32)
    ebias = sb("ebias", [128, 1], F32)
    attnb = [sb(f"attnb{i}", [128, DC * 128], BF16) for i in range(2)]
    wmb = [attnb[i][:, 0:768] for i in range(2)]
    attn0 = sb("attn0", [128, DC * 128], BF16)
    m0keep = sb("m0keep", [128, 16], F32)
    s0keep = sb("s0keep", [128, 16], F32)
    x2stage = [sb(f"x2st{i}", [128, D], BF16) for i in range(2)]
    h2f = sb("h2f", [128, D], F32)

    l0ch = sb("l0ch", [128, GSZ], F32)
    m0far = sb("m0far", [128, 1], F32)
    m0tmp = sb("m0tmp", [128, 1], F32)
    rfar = sb("rfar", [128, 1], F32)
    s0far = sb("s0far", [128, 1], F32)
    spart = sb("spart", [128, 1], F32)
    sacc = sb("sacc", [128, D], F32)
    saccb = sb("saccb", [128, D], BF16)
    sTf = sb("sTf", [128, 128], BF16)
    p16 = sb("p16", [128, GSZ], BF16)
    p16T = sb("p16T", [128, GSZ], BF16)
    q0far = sb("q0far", [128, 8], F32)
    rows0 = sb("rows0", [1, 112], F32)
    ebf = sb("ebf", [128, 8], BF16)
    ebf2 = sb("ebf2", [128, 8], BF16)
    colA = sb("colA", [128, 8], F32)
    colF = sb("colF", [128, 8], F32)
    colR = sb("colR", [128, 8], F32)
    ctmp = sb("ctmp", [128, 8], F32)
    coltmp = sb("coltmp", [128, 1], F32)

    NBW = D // 16
    qt_bm = sb("qt_bm", [128, NBW], F32)
    qt_s6 = sb("qt_s6", [128, NBW], F32)
    qt_rcp = sb("qt_rcp", [128, NBW], F32)
    qt_s16 = sb("qt_s16", [128, NBW], BF16)
    qt_xn = sb("qt_xn", [128, D], F32)
    qt_m0 = sb("qt_m0", [128, D], F32)
    qt_btb = sb("qt_btb", [128, D], BF16)
    qt_sg = sb("qt_sg", [128, D], BF16)
    qt_s1 = sb("qt_s1", [128, D], BF16)
    qt_s2 = sb("qt_s2", [128, D], BF16)
    wqt = sb("wqt", [128, D], BF16)
    h2qb = sb("h2qb", [128, D], BF16)
    h2qtt = sb("h2qtt", [128, D], BF16)
    xn0m = wqt
    v0row = h2qb

    wgC = wfm["wq"]
    wuC = wfm["wk"]
    wdCb = wfm["wv"]
    oacc = ([xbf[:, 0:6144].bitcast(F32)[:, 1024 * i:1024 * i + 1024] for i in range(3)]
            + [xnT[:, 0:6144].bitcast(F32)[:, 1024 * i:1024 * i + 1024] for i in range(3)]
            + [xn_tm[:, 0:4096].bitcast(F32)[:, 1024 * i:1024 * i + 1024] for i in range(2)])
    h2qTb = ([Vs[i][:, 0:D] for i in range(NKV)] + [wqt[:, 0:D], h2qb[:, 0:D],
              h2qtt[:, 0:D]])
    xstb = xst[0][:, 0:1024].bitcast(BF16)
    gsil = xstb[:, 0:512]
    hmq = xstb[:, 512:1024]
    hmqT = xstb[:, 1024:1536]

    ps = nc.alloc_psum_tensor
    P_mm = ps("P_mm", [128, 512], F32)
    P_mm2 = ps("P_mm2", [128, 512], F32)
    P_tr = ps("P_tr", [128, 1024], BF16)
    P_sc = ps("P_sc", [128, 512], F32)
    P_av = ps("P_av", [128, 384], F32)
    P_tr2 = ps("P_tr2", [128, 1024], BF16)
    P_o = ps("P_o", [128, 512], F32)
    P_x = ps("P_x", [128, 512], F32)

    SY, TE, VE, SE, GE = nc.sync, nc.tensor, nc.vector, nc.scalar, nc.gpsimd

    class Sem:
        __slots__ = ("h", "n")

        def __init__(self, name):
            self.h = nc.alloc_semaphore(name)
            self.n = 0

        def inc(self, inst, k=1):
            inst.then_inc(self.h, k)
            self.n += k
            return self.n

    s_ld = Sem("s_ld"); s_x = Sem("s_x"); s_mt = Sem("s_mt"); s_v = Sem("s_v")
    s_s = Sem("s_s"); s_g = Sem("s_g"); s_st = Sem("s_st"); s_sp = Sem("s_sp")
    s_w2 = Sem("s_w2"); s_wl = Sem("s_wl")

    waited = {}

    def W(eng, sem, val=None):
        val = sem.n if val is None else val
        key = (id(eng), id(sem))
        if waited.get(key, -1) >= val or val <= 0:
            return
        eng.wait_ge(sem.h, val)
        waited[key] = val

    pfree = {}

    def puse(pt):
        if id(pt) in pfree:
            sem, val = pfree[id(pt)]
            W(TE, sem, val)

    def pdone(pt, sem):
        pfree[id(pt)] = (sem, sem.n)

    MM = TE.matmul

    def pe_flush():
        puse(P_tr2)
        s_mt.inc(TE.transpose(P_tr2[:, 0:128], ident[:], ident[:]), 1)
        s_mt.inc(TE.transpose(P_tr2[:, 128:256], ident[:], ident[:]), 1)
        pdone(P_tr2, s_mt)

    def preload_set(set_name):
        from concourse.hw_specs import get_activation_tables
        names = list(get_activation_tables(nc.m.arch).keys())
        idx = names.index(set_name)
        SE.add_instruction(mybir.InstLoadActFuncSet(
            name=nc.get_next_instruction_name(), act_func_set_id=idx,
            ins=[], outs=[]))

    s_ld.inc(SY.dma_start(c_fm[:], cfm_in), 16)
    s_ld.inc(SY.dma_start(ident[:], ident_in), 16)
    s_ld.inc(SY.dma_start(e0[:], e0_in), 16)
    s_ld.inc(SY.dma_start(halfm[:], hm_in), 16)

    s_ld.inc(SY.dma_start(epar[:], ep_in), 16)
    s_ld.inc(SY.dma_start(qg[:], qg_in), 16)
    s_ld.inc(SY.dma_start(x0row[0:1, :], x0_in), 16)
    LD_CONST = s_ld.n

    for t_ in upk + k0pk + v0hm + q0pk + [padT, padC, v0row, p0T, p16T, p16, sacc,
                                   saccb, sTf, l0buf, qe, qo, ebf, ebf2, rows0]:
        GE.memset(t_[:], 0.0)
    GE.memset(parmask[:], 0.0)
    GE.memset(m0far[:], NEG)
    s_g.inc(GE.memset(s0far[:], 0.0), 1)
    GZ = s_g.n
    W(SY, s_g, GZ)
    s_ld.inc(SY.dma_start(parmask[0:16, :], pm_in), 16)

    preload_set("silu_and_others")
    W(SE, s_ld, 16)
    s_s.inc(SE.activation(silc[:], c_fm[:], AF.Silu), 1)
    SIL = s_s.n
    preload_set("natural_log_exp_and_others")

    W(TE, s_s, SIL)
    first_mm = True
    wm_cast = {}
    for kk in range(DC * 8):
        k, h8 = divmod(kk, 8)
        slot = kk % 2
        if kk >= 2:
            W(SY, s_v, wm_cast[kk - 2])
        s_ld.inc(SY.dma_start(wst[slot][:, 0:768],
                              wmod_in[128 * k:128 * k + 128, 768 * h8:768 * h8 + 768]), 16)
        W(VE, s_ld, s_ld.n)
        if kk >= 2:
            W(VE, s_mt, s_mt.n)
        s_v.inc(VE.tensor_copy(wmb[slot], wst[slot][:, 0:768]), 1)
        wm_cast[kk] = s_v.n
        W(TE, s_v, s_v.n)
        for m6 in range(6):
            m = h8 * 6 + m6
            ii = MM(P_mm[:, m:m + 1], wmb[slot][:, 128 * m6:128 * m6 + 128],
                    silc[:, k:k + 1], start=first_mm,
                    stop=(kk == DC * 8 - 1 and m6 == 5))
            first_mm = False
        s_mt.inc(ii, 1)
    pe_flush()
    W(VE, s_mt, s_mt.n)
    s_v.inc(VE.tensor_copy(mod_fm[:], P_mm[:, 0:48]), 1)
    pdone(P_mm, s_v)
    MOD_EV = s_v.n
    VE.drain()
    s_v.inc(VE.tensor_scalar(ca1[:], mod_fm[:, 8:16], 1.0, None, A.add), 1)
    s_v.inc(VE.tensor_scalar(cb1[:], mod_fm[:, 32:40], 1.0, None, A.add), 1)
    s_v.inc(VE.tensor_copy(sab[:], mod_fm[:, 0:8]), 1)
    for (src, dstm) in ((16, gam), (40, gbm)):
        VE.tensor_scalar(ctmp[:], mod_fm[:, src:src + 8], 0.0, 2.0, A.is_ge, A.mult)
        VE.drain()
        VE.tensor_scalar(ctmp[:], ctmp[:], -1.0, None, A.add)
        VE.drain()
        VE.tensor_scalar(dstm[:].bitcast(I32), mod_fm[:, src:src + 8].bitcast(I32),
                         0x7FFFFFFF, None, A.bitwise_and)
        VE.drain()
        VE.tensor_scalar(dstm[:], dstm[:], 0.25, None, A.max)
        VE.drain()
        s_v.inc(VE.tensor_mul(dstm[:], dstm[:], ctmp[:]), 1)
    VE.drain()

    def bcast_build(vec_ap, dest):
        for c in range(DC):
            s_v.inc(VE.tensor_copy(
                padC[:], vec_ap[:, c:c + 1].broadcast_to([128, 128])), 1)
            VE.drain()
            W(TE, s_v, s_v.n)
            puse(P_tr2)
            s_mt.inc(TE.transpose(P_tr2[:, 0:128], padC[:], ident[:]), 1)
            W(VE, s_mt, s_mt.n)
            s_v.inc(VE.tensor_copy(dest[:, 128 * c:128 * c + 128], P_tr2[:, 0:128]), 1)
            pdone(P_tr2, s_v)

    bcast_build(ca1[:], ca1B)
    bcast_build(gam[:], gaB)
    bcast_build(gbm[:], gbB)
    bcast_build(cb1[:], cb1B)
    bcast_build(mod_fm[:, 24:32], sbB)
    BCAST1 = s_v.n

    def wprep(name, hbm, with_u=False):
        for k in range(DC):
            kk = wp_ctr[0]; wp_ctr[0] += 1
            slot = kk % 2
            if kk >= 2:
                W(SY, s_v, wp_cast[kk - 2])
            s_ld.inc(SY.dma_start(wst[slot][:], hbm[128 * k:128 * k + 128, :]), 16)
            W(VE, s_ld, s_ld.n)
            W(VE, s_mt, s_mt.n)
            s_v.inc(VE.tensor_copy(wbf[:], wst[slot][:]), 1)
            wp_cast[kk] = s_v.n
            W(TE, s_v, s_v.n)
            puse(P_tr)
            for i in range(DC):
                ii = TE.transpose(P_tr[:, 128 * i:128 * i + 128],
                                  wbf[:, 128 * i:128 * i + 128], ident[:])
            s_mt.inc(ii, 1)
            if with_u:
                if k == 0:
                    puse(P_x)
                for i in range(DC):
                    ii = MM(P_x[:, 16 * i:16 * i + 16], wbf[:, 128 * i:128 * i + 128],
                            q0pk[k][:, 0:16], start=(k == 0 and i == 0),
                            stop=(k == DC - 1 and i == DC - 1))
                if k == DC - 1:
                    s_mt.inc(ii, 1)
            W(VE, s_mt, s_mt.n)
            for i in range(DC):
                ii = VE.tensor_copy(wfm[name][i][:, 128 * k:128 * k + 128],
                                    P_tr[:, 128 * i:128 * i + 128])
            s_v.inc(ii, 1)
            pdone(P_tr, s_v)

    def bias_build(name, dest):
        W(TE, s_v, s_v.n)
        puse(P_mm2)
        first = True
        for oc in range(DC):
            for i in range(DC):
                ii = MM(P_mm2[:, oc:oc + 1], wfm[name][i][:, 128 * oc:128 * oc + 128],
                        sab[:, i:i + 1], start=first,
                        stop=(oc == DC - 1 and i == DC - 1))
                first = False
        s_mt.inc(ii, 1)
        pe_flush()
        W(VE, s_mt, s_mt.n)
        s_v.inc(VE.tensor_copy(dest[:], P_mm2[:, 0:8]), 1)
        pdone(P_mm2, s_v)

    wp_cast = {}
    wp_ctr = [0]
    wprep("wq", w_in["wq"])
    bias_build("wq", bqv)

    W(VE, s_ld, LD_CONST)
    VE.bn_stats(mvst[0:1, 0:6], x0row[0:1, 0:512])
    VE.bn_stats(mvst[0:1, 6:12], x0row[0:1, 512:1024])
    VE.drain()
    VE.bn_aggr(mva[0:1, :], mvst[0:1, 0:12])
    VE.drain()
    VE.tensor_scalar(vtmp[0:1, :], mva[0:1, 1:2], EPS, None, A.add)
    VE.drain()
    s_v.inc(VE.reciprocal(rstd[0:1, :], vtmp[0:1, :]), 1)
    W(SE, s_v, s_v.n)
    SE.activation(vtmp[0:1, :], rstd[0:1, :], AF.Ln)
    SE.drain()
    s_s.inc(SE.activation(rstd[0:1, :], vtmp[0:1, :], AF.Exp, scale=0.5), 1)
    W(VE, s_s, s_s.n)
    s_v.inc(VE.scalar_tensor_tensor(nbias[0:1, :], mva[0:1, 0:1], -1.0, rstd[0:1, :],
                                    A.mult, A.mult), 1)
    W(SE, s_v, s_v.n)
    s_s.inc(SE.activation(xn0m[0:1, :], x0row[0:1, :], AF.Identity,
                          bias=nbias[0:1, :], scale=rstd[0:1, :]), 1)
    W(VE, s_s, s_s.n)
    W(VE, s_v, BCAST1)
    s_v.inc(VE.tensor_mul(xn0m[0:1, :], xn0m[0:1, :], ca1B[0:1, :]), 1)
    VE.drain()
    for c in range(DC):
        s_v.inc(VE.tensor_copy(padT[0:1, 0:128], xn0m[0:1, 128 * c:128 * c + 128]), 1)
        VE.drain()
        W(TE, s_v, s_v.n)
        puse(P_tr2)
        s_mt.inc(TE.transpose(P_tr2[:, 0:128], padT[:], ident[:]), 1)
        W(VE, s_mt, s_mt.n)
        s_v.inc(VE.tensor_copy(x0f[:, c:c + 1], P_tr2[:, 0:1]), 1)
        pdone(P_tr2, s_v)
    VE.drain()

    def vec_proj(name, bias_ap, dest_bf):
        W(TE, s_v, s_v.n)
        puse(P_sc)
        first = True
        for oc in range(DC):
            for i in range(DC):
                ii = MM(P_sc[:, oc:oc + 1], wfm[name][i][:, 128 * oc:128 * oc + 128],
                        x0f[:, i:i + 1], start=first,
                        stop=(oc == DC - 1 and i == DC - 1))
                first = False
        s_mt.inc(ii, 1)
        pe_flush()
        W(VE, s_mt, s_mt.n)
        s_v.inc(VE.tensor_add(dest_bf[:], P_sc[:, 0:8], bias_ap), 1)
        pdone(P_sc, s_v)
        VE.drain()

    vec_proj("wq", bqv[:], q0f)
    W(VE, s_g, GZ)
    for cc in range(DC):
        VE.tensor_mul(q0pk[cc][:, 2 * cc:2 * cc + 1], q0f[:, cc:cc + 1], halfm[:, 0:1])
        s_v.inc(VE.tensor_mul(q0pk[cc][:, 2 * cc + 1:2 * cc + 2], q0f[:, cc:cc + 1],
                              halfm[:, 1:2]), 1)
    VE.drain()

    wprep("wk", w_in["wk"], with_u=True)
    bias_build("wk", bkv)
    W(VE, s_mt, s_mt.n)
    s_v.inc(VE.tensor_copy(ufm[:], P_x[:, 0:128]), 1)
    pdone(P_x, s_v)
    s_v.inc(VE.tensor_copy(bkb[:], bkv[:]), 1)
    VE.drain()
    vec_proj("wk", bkv[:], k0f)

    W(TE, s_v, s_v.n)
    puse(P_x)
    for oc in range(DC):
        s_mt.inc(MM(P_x[:, 0:1], q0pk[oc][:], bkb[:, oc:oc + 1], start=(oc == 0),
                    stop=(oc == DC - 1)), 1)
    pe_flush()
    W(VE, s_mt, s_mt.n)
    s_v.inc(VE.tensor_copy(c0col[:], P_x[:, 0:1]), 1)
    pdone(P_x, s_v)

    W(VE, s_g, GZ)
    for c in range(DC):
        VE.tensor_copy(upk[c][:, 0:16], ufm[:, 16 * c:16 * c + 16])
        VE.tensor_mul(k0pk[c][:, 2 * c:2 * c + 1], k0f[:, c:c + 1], halfm[:, 0:1])
        s_v.inc(VE.tensor_mul(k0pk[c][:, 2 * c + 1:2 * c + 2], k0f[:, c:c + 1],
                              halfm[:, 1:2]), 1)
    VE.drain()
    UPK = s_v.n

    wprep("wv", w_in["wv"])
    bias_build("wv", bvv)
    bcast_build(bvv[:], bvB)
    for half in range(2):
        first = True
        for i in range(DC):
            W(VE, s_mt, s_mt.n)
            s_v.inc(VE.tensor_copy(padC[:, 0:1], x0f[:, i:i + 1]), 1)
            VE.drain()
            W(TE, s_v, s_v.n)
            if first:
                puse(P_o)
            s_mt.inc(MM(P_o[:, 0:512], padC[:],
                        wfm["wv"][i][:, 512 * half:512 * half + 512],
                        start=first, stop=(i == DC - 1)), 1)
            first = False
        pe_flush()
        W(VE, s_mt, s_mt.n)
        s_v.inc(VE.tensor_add(v0row[0:1, 512 * half:512 * half + 512],
                              P_o[0:1, 0:512], bvB[0:1, 512 * half:512 * half + 512]), 1)
        pdone(P_o, s_v)
    VE.drain()
    for c in range(DC):
        s_v.inc(VE.tensor_copy(padT[0:1, 0:128], v0row[0:1, 128 * c:128 * c + 128]), 1)
        VE.drain()
        W(TE, s_v, s_v.n)
        puse(P_tr2)
        s_mt.inc(TE.transpose(P_tr2[:, 0:128], padT[:], ident[:]), 1)
        W(VE, s_mt, s_mt.n)
        s_v.inc(VE.tensor_copy(v0f[:, c:c + 1], P_tr2[:, 0:1]), 1)
        pdone(P_tr2, s_v)
    VE.drain()
    for c in range(DC):
        s_v.inc(VE.memset(padC[:], 0.0), 1)
        VE.drain()
        VE.tensor_copy(padC[0:64, 2 * c:2 * c + 1], v0f[0:64, c:c + 1])
        s_v.inc(VE.tensor_copy(padC[64:128, 2 * c + 1:2 * c + 2], v0f[64:128, c:c + 1]), 1)
        VE.drain()
        W(TE, s_v, s_v.n)
        puse(P_tr2)
        s_mt.inc(TE.transpose(P_tr2[:, 0:128], padC[:], ident[:]), 1)
        W(VE, s_mt, s_mt.n)
        s_v.inc(VE.tensor_copy(v0hm[c][:], P_tr2[:, 0:128]), 1)
        pdone(P_tr2, s_v)
    wprep("wo", w_in["wo"])
    PRO_DONE = s_v.n

    if "mod_fm" in taps:
        W(SY, s_v, MOD_EV)
        s_st.inc(SY.dma_start(tap("mod_fm", [128, 48]), mod_fm[:]), 16)
    if "q0f" in taps:
        W(SY, s_v, PRO_DONE)
        s_st.inc(SY.dma_start(tap("q0f", [128, 8], BF16), q0f[:]), 16)
    if "uf" in taps:
        W(SY, s_v, PRO_DONE)
        s_st.inc(SY.dma_start(tap("uf", [128, 128], BF16), ufm[:]), 16)
    if "xn0m" in taps:
        W(SY, s_v, PRO_DONE)
        s_st.inc(SY.dma_start(tap("xn0m", [1, D], BF16), xn0m[0:1, :]), 16)
    if "x0f" in taps:
        W(SY, s_v, PRO_DONE)
        s_st.inc(SY.dma_start(tap("x0f", [128, 8], BF16), x0f[:]), 16)
    if "bqv" in taps:
        W(SY, s_v, PRO_DONE)
        s_st.inc(SY.dma_start(tap("bqv", [128, 8]), bqv[:]), 16)
    if "ca1" in taps:
        W(SY, s_v, PRO_DONE)
        s_st.inc(SY.dma_start(tap("ca1", [128, 8]), ca1[:]), 16)
    if "wqf0" in taps:
        W(SY, s_v, PRO_DONE)
        s_st.inc(SY.dma_start(tap("wqf0", [128, D], BF16), wfm["wq"][0][:]), 16)
    if "sabt" in taps:
        W(SY, s_v, PRO_DONE)
        s_st.inc(SY.dma_start(tap("sabt", [128, 8], BF16), sab[:]), 16)
    if "v0row" in taps:
        W(SY, s_v, PRO_DONE)
        s_st.inc(SY.dma_start(tap("v0row", [1, D], BF16), v0row[0:1, :]), 16)

    if "gbBd" in taps:
        taps = tuple(taps) + ("wdqd", "h2qd", "wgqd", "wuqd", "x2sd")
        W(SY, s_st, s_st.n)
        s_st.inc(SY.dma_start(tap("gbBd", [128, D], BF16), gbB[:]), 16)
        s_st.inc(SY.dma_start(tap("wdqd", [IFF, D], BF16), wdq_h), 16)
        s_st.inc(SY.dma_start(tap("h2qd", [D, T_OWN], BF16), h2qT_h), 16)
        s_st.inc(SY.dma_start(tap("wgqd", [D, IFF], BF16), wgq_h), 16)
        s_st.inc(SY.dma_start(tap("wuqd", [D, IFF], BF16), wuq_h), 16)
        s_st.inc(SY.dma_start(tap("x2sd", [T_OWN, D], BF16), x2s), 16)

    def finalize():
        for sem in (s_mt, s_v, s_s, s_g):
            W(SY, sem, sem.n)
        W(SY, s_st, s_st.n)
        W(SY, s_sp, s_sp.n)
        return nc, dbg

    if stage == 1:
        return finalize()

    xld = {}
    xst_guard = {0: [(s_v, PRO_DONE)]}

    def load_xtile(lt):
        slot = lt % 2
        for sem, val in xst_guard.get(slot, ()):
            W(SY, sem, val)
        if lt < NTL:
            src = x_loc[128 * lt:128 * lt + 128, :]
        else:
            f = lt - NTL
            src = x_far[128 * f:128 * f + 128, :]
        s_x.inc(SY.dma_start(xst[slot][:], src), 16)
        xld[lt] = s_x.n

    def ln_tile(lt, g, w, own):
        slot = lt % 2
        W(SE, s_v, PRO_DONE)
        W(VE, s_x, xld[lt])
        W(VE, s_s, s_s.n)
        VE.bn_stats(mvst[:, 0:6], xst[slot][:, 0:512])
        VE.bn_stats(mvst[:, 6:12], xst[slot][:, 512:1024])
        VE.drain()
        VE.bn_aggr(mva[:], mvst[:, 0:12])
        VE.drain()
        VE.tensor_scalar(vtmp[:], mva[:, 1:2], EPS, None, A.add)
        VE.drain()
        s_v.inc(VE.reciprocal(rstd[:], vtmp[:]), 1)
        W(SE, s_v, s_v.n)
        SE.activation(vtmp[:], rstd[:], AF.Ln)
        SE.drain()
        s_s.inc(SE.activation(rstd[:], vtmp[:], AF.Exp, scale=0.5), 1)
        W(VE, s_s, s_s.n)
        s_v.inc(VE.scalar_tensor_tensor(nbias[:], mva[:, 0:1], -1.0, rstd[:],
                                        A.mult, A.mult), 1)
        W(SE, s_v, s_v.n)
        W(SE, s_mt, s_mt.n)
        s_s.inc(SE.activation(xn_tm[:, D * (lt % 4):D * (lt % 4) + D], xst[slot][:],
                              AF.Identity, bias=nbias[:], scale=rstd[:]), 1)
        XN = s_s.n
        if own:
            s_v.inc(VE.tensor_copy(xbf[:, D * (lt % 6):D * (lt % 6) + D],
                                   xst[slot][:]), 1)
            xst_guard[slot] = [(s_v, s_v.n), (s_s, XN)]
        else:
            xst_guard[slot] = [(s_s, XN)]
        W(TE, s_s, XN)
        puse(P_tr)
        for c in range(DC):
            ii = TE.transpose(P_tr[:, 128 * c:128 * c + 128],
                              xn_tm[:, D * (lt % 4) + 128 * c:D * (lt % 4) + 128 * c + 128],
                              ident[:])
        s_mt.inc(ii, 1)
        W(SE, s_mt, s_mt.n)
        gb_ = g % 2
        for c in range(DC):
            ii = SE.activation(
                xnT[:, gb_ * DC * GSZ + c * GSZ + 128 * w:
                     gb_ * DC * GSZ + c * GSZ + 128 * w + 128],
                P_tr[:, 128 * c:128 * c + 128], AF.Identity, scale=ca1[:, c:c + 1])
        s_s.inc(ii, 1)
        pdone(P_tr, s_s)

    qkv_guard = {}

    def qkv_group(g):
        gb_ = g % 2
        base = gb_ * DC * GSZ
        slot0 = (3 * g) % NKV
        W(TE, s_s, s_s.n)
        for dst, wname, bias in ((Qs, "wq", bqv), (Ks, "wk", bkv)):
            for c in range(DC):
                pt = P_mm if c % 2 == 0 else P_mm2
                puse(pt)
                for i in range(DC):
                    ii = MM(pt[:, 0:GSZ], wfm[wname][i][:, 128 * c:128 * c + 128],
                            xnT[:, base + i * GSZ:base + i * GSZ + GSZ],
                            start=(i == 0), stop=(i == DC - 1))
                s_mt.inc(ii, 1)
                W(SE, s_mt, s_mt.n)
                for w in range(3):
                    sl = (3 * g + w) % NKV
                    ii = SE.activation(dst[c][:, 128 * sl:128 * sl + 128],
                                       pt[:, 128 * w:128 * w + 128],
                                       AF.Identity, bias=bias[:, c:c + 1])
                s_s.inc(ii, 1)
                pdone(pt, s_s)
        for w in range(3):
            for half in range(2):
                pt = P_mm if half == 0 else P_mm2
                puse(pt)
                for i in range(DC):
                    ii = MM(pt[:, 0:512],
                            xnT[:, base + i * GSZ + 128 * w:base + i * GSZ + 128 * w + 128],
                            wfm["wv"][i][:, 512 * half:512 * half + 512],
                            start=(i == 0), stop=(i == DC - 1))
                s_mt.inc(ii, 1)
                W(VE, s_mt, s_mt.n)
                s_v.inc(VE.tensor_add(Vs[(3 * g + w) % NKV][:, 512 * half:512 * half + 512],
                                      pt[:, 0:512], bvB[:, 512 * half:512 * half + 512]), 1)
                pdone(pt, s_v)
        W(TE, s_v, UPK)
        W(TE, s_s, s_s.n)
        puse(P_sc)
        for w in range(3):
            sl = (3 * g + w) % NKV
            for c in range(DC):
                ii = MM(P_sc[:, 128 * w:128 * w + 128], k0pk[c][:],
                        Qs[c][:, 128 * sl:128 * sl + 128],
                        start=(c == 0), stop=(c == DC - 1))
        s_mt.inc(ii, 1)
        W(VE, s_mt, s_mt.n)
        s_v.inc(VE.tensor_copy(l0st[:, 0:GSZ], P_sc[:, 0:GSZ]), 1)
        pdone(P_sc, s_v)
        W(TE, s_v, s_v.n)
        for w in range(3):
            puse(P_tr2)
            s_mt.inc(TE.transpose(P_tr2[:, 0:128], l0st[:, 128 * w:128 * w + 128],
                                  ident[:]), 1)
            W(VE, s_mt, s_mt.n)
            lt = 3 * g + w
            s_v.inc(VE.tensor_copy(l0buf[:, 16 * lt:16 * lt + 16], P_tr2[:, 0:16]), 1)
            pdone(P_tr2, s_v)

    def far_group(g):
        gb_ = g % 2
        base = gb_ * DC * GSZ
        W(TE, s_v, UPK)
        W(TE, s_s, s_s.n)
        puse(P_x)
        for c in range(DC):
            ii = MM(P_x[:, 0:GSZ], upk[c][:], xnT[:, base + c * GSZ:base + c * GSZ + GSZ],
                    start=(c == 0), stop=(c == DC - 1))
        s_mt.inc(ii, 1)
        W(VE, s_mt, s_mt.n)
        s_v.inc(VE.tensor_scalar(l0ch[0:16, :], P_x[0:16, 0:GSZ],
                                 c0col[0:16, 0:1], None, A.add), 1)
        pdone(P_x, s_v)
        s_v.inc(VE.reduce_max(m0tmp[0:16, :], l0ch[0:16, :], axis=AX), 1)
        VE.drain()
        VE.tensor_max(m0tmp[0:16, :], m0tmp[0:16, :], m0far[0:16, :])
        VE.drain()
        VE.tensor_sub(rfar[0:16, :], m0far[0:16, :], m0tmp[0:16, :])
        VE.drain()
        s_v.inc(VE.tensor_copy(m0far[0:16, :], m0tmp[0:16, :]), 1)
        W(SE, s_v, s_v.n)
        s_s.inc(SE.activation(rfar[0:16, :], rfar[0:16, :], AF.Exp, scale=SCALE), 1)
        W(VE, s_s, s_s.n)
        VE.tensor_mul(s0far[0:16, :], s0far[0:16, :], rfar[0:16, :])
        s_v.inc(VE.tensor_scalar(sacc[0:16, :], sacc[0:16, :], rfar[0:16, :],
                                 None, A.mult), 1)
        VE.drain()
        s_v.inc(VE.tensor_scalar(m0tmp[0:16, :], m0tmp[0:16, :], -SCALE, None, A.mult), 1)
        VE.drain()
        W(SE, s_v, s_v.n)
        SE.activation(p16[0:16, :], l0ch[0:16, :], AF.Exp,
                      bias=m0tmp[0:16, :], scale=SCALE,
                      accum_out=spart[0:16, :])
        s_s.inc(SE.drain(), 1)
        W(VE, s_s, s_s.n)
        s_v.inc(VE.tensor_add(s0far[0:16, :], s0far[0:16, :], spart[0:16, :]), 1)
        W(TE, s_s, s_s.n)
        for w in range(3):
            puse(P_tr2)
            s_mt.inc(TE.transpose(P_tr2[:, 0:128], p16[:, 128 * w:128 * w + 128],
                                  ident[:]), 1)
            W(VE, s_mt, s_mt.n)
            s_v.inc(VE.tensor_copy(p16T[:, 128 * w:128 * w + 128], P_tr2[:, 0:128]), 1)
            pdone(P_tr2, s_v)
        W(TE, s_v, s_v.n)
        for half in range(2):
            pt = P_mm if half == 0 else P_mm2
            puse(pt)
            for w in range(3):
                lt = 3 * g + w
                ii = MM(pt[:, 0:512], p16T[:, 128 * w:128 * w + 128],
                        xn_tm[:, D * (lt % 4) + 512 * half:D * (lt % 4) + 512 * half + 512],
                        start=(w == 0), stop=(w == 2))
            s_mt.inc(ii, 1)
            W(VE, s_mt, s_mt.n)
            s_v.inc(VE.tensor_add(sacc[0:16, 512 * half:512 * half + 512],
                                  sacc[0:16, 512 * half:512 * half + 512],
                                  pt[0:16, 0:512]), 1)
            pdone(pt, s_v)
        VE.drain()

    def quant_fp4(src_f32, nb, dst_bf):
        n = 16 * nb
        W(VE, s_s, s_s.n)
        s_v.inc(VE.tensor_reduce(qt_bm[:, 0:nb],
                                 src_f32.rearrange("p (nb k) -> p nb k", k=16),
                                 AX, A.max, apply_absolute_value=True), 1)
        VE.drain()
        VE.tensor_scalar(qt_s6[:, 0:nb], qt_bm[:, 0:nb], 1.0 / 6.0, 1e-12,
                         A.mult, A.max)
        VE.drain()
        VE.reciprocal(qt_rcp[:, 0:nb], qt_s6[:, 0:nb])
        VE.tensor_copy(qt_s16[:, 0:nb], qt_s6[:, 0:nb])
        VE.drain()
        s_v.inc(VE.tensor_tensor(
            qt_xn[:, 0:n].rearrange("p (nb k) -> p nb k", k=16),
            src_f32.rearrange("p (nb k) -> p nb k", k=16),
            qt_rcp[:, 0:nb].unsqueeze(2).broadcast_to([128, nb, 16]), A.mult), 1)
        XN_ = s_v.n
        W(SE, s_v, XN_)
        s_s.inc(SE.sign(qt_sg[:, 0:n], qt_xn[:, 0:n]), 1)
        VE.tensor_scalar(qt_m0[:, 0:n].bitcast(I32), qt_xn[:, 0:n].bitcast(I32),
                         0x7FFFFFFF, None, A.bitwise_and)
        VE.tensor_scalar(qt_s1[:, 0:n], qt_m0[:, 0:n], 0.25, 0.5, A.is_gt, A.mult)
        VE.tensor_scalar(qt_s2[:, 0:n], qt_m0[:, 0:n], 0.75, 0.5, A.is_gt, A.mult)
        VE.tensor_scalar(qt_m0[:, 0:n], qt_m0[:, 0:n], 1.0, None, A.max)
        VE.tensor_scalar(qt_m0[:, 0:n].bitcast(I32), qt_m0[:, 0:n].bitcast(I32),
                         0x001FFFFF, None, A.add)
        VE.tensor_scalar(qt_m0[:, 0:n].bitcast(I32), qt_m0[:, 0:n].bitcast(I32),
                         -4194304, None, A.bitwise_and)
        VE.tensor_add(qt_s1[:, 0:n], qt_s1[:, 0:n], qt_s2[:, 0:n])
        W(VE, s_s, s_s.n)
        VE.tensor_mul(qt_btb[:, 0:n], qt_m0[:, 0:n], qt_sg[:, 0:n])
        VE.tensor_mul(qt_btb[:, 0:n], qt_btb[:, 0:n], qt_s1[:, 0:n])
        s_v.inc(VE.tensor_tensor(
            dst_bf.rearrange("p (nb k) -> p nb k", k=16),
            qt_btb[:, 0:n].rearrange("p (nb k) -> p nb k", k=16),
            qt_s16[:, 0:nb].unsqueeze(2).broadcast_to([128, nb, 16]), A.mult), 1)

    wq_cast = {}
    wtile_idx = [0]
    wbf_spill = [0]

    def ffnw_tile2(which, hbm_in, hbm_out, t, part=None):
        if which in ("wg", "wu"):
            width, nb = D, D // 16
            src = hbm_in[128 * t:128 * t + 128, :]
            slot = wtile_idx[0] % 2
            if wtile_idx[0] >= 2:
                W(SY, s_v, wq_cast[wtile_idx[0] - 2])
            s_wl.inc(SY.dma_start(wst[slot][:, 0:width], src), 16)
            W(VE, s_wl, s_wl.n)
            quant_fp4(wst[slot][:, 0:width], nb, wqt[:, 0:width])
            wq_cast[wtile_idx[0]] = s_v.n
            wtile_idx[0] += 1
            W(TE, s_v, s_v.n)
            puse(P_tr)
            for i in range(DC):
                ii = TE.transpose(P_tr[:, 128 * i:128 * i + 128],
                                  wqt[:, 128 * i:128 * i + 128], ident[:])
            s_mt.inc(ii, 1)
            W(VE, s_mt, s_mt.n)
            W(VE, s_sp, wbf_spill[0])
            s_v.inc(VE.tensor_copy(wbf[:], P_tr[:, 0:1024]), 1)
            pdone(P_tr, s_v)
            W(SY, s_v, s_v.n)
            for i in range(DC):
                s_sp.inc(SY.dma_start(
                    hbm_out[128 * i:128 * i + 128, 128 * t:128 * t + 128],
                    wbf[:, 128 * i:128 * i + 128]), 16)
            wbf_spill[0] = s_sp.n
        else:
            for ph in range(2):
                nb = 1408 // 16
                src = hbm_in[128 * t:128 * t + 128, 1408 * ph:1408 * ph + 1408]
                slot = wtile_idx[0] % 2
                if wtile_idx[0] >= 2:
                    W(SY, s_v, wq_cast[wtile_idx[0] - 2])
                for pp, (o0, ww) in enumerate(((0, 1024), (1024, 384))):
                    W(SY, s_v, s_v.n)
                    s_wl.inc(SY.dma_start(wst[slot][:, 0:ww],
                                          src[:, o0:o0 + ww]), 16)
                    W(VE, s_wl, s_wl.n)
                    quant_fp4(wst[slot][:, 0:ww], ww // 16, wqt[:, 0:ww])
                    wq_cast[wtile_idx[0]] = s_v.n
                    W(TE, s_v, s_v.n)
                    puse(P_tr)
                    nblk = ww // 128
                    for i in range(nblk):
                        ii = TE.transpose(P_tr[:, 128 * i:128 * i + 128],
                                          wqt[:, 128 * i:128 * i + 128], ident[:])
                    s_mt.inc(ii, 1)
                    W(VE, s_mt, s_mt.n)
                    W(VE, s_sp, wbf_spill[0])
                    s_v.inc(VE.tensor_copy(wbf[:, 0:ww], P_tr[:, 0:ww]), 1)
                    pdone(P_tr, s_v)
                    W(SY, s_v, s_v.n)
                    for i in range(nblk):
                        r0 = 1408 * ph + o0 + 128 * i
                        s_sp.inc(SY.dma_start(
                            wdq_h[r0:r0 + 128, 128 * t:128 * t + 128],
                            wbf[:, 128 * i:128 * i + 128]), 16)
                    wbf_spill[0] = s_sp.n
                wtile_idx[0] += 1

    ffnw_items = ([("wg", wg_in, wgq_h, t) for t in range(NGW)]
                  + [("wu", wu_in, wuq_h, t) for t in range(NGW)]
                  + [("wd", wd_in, wdq_h, t) for t in range(DC)])
    ffnw_pos = [0]

    def ffnw_some(k):
        for _ in range(k):
            if ffnw_pos[0] < len(ffnw_items):
                ffnw_tile2(*ffnw_items[ffnw_pos[0]])
                ffnw_pos[0] += 1

    x2_guard = {}
    x2_spill = {}
    mask_guard = {}
    mask_ld = {}

    def attend(a):
        lt = a + 1
        qslot = lt % NKV
        out_t = attn0 if a == 0 else attnb[a % 2]
        ms = a % 2
        if ms in mask_guard:
            W(SY, s_v, mask_guard[ms])
        s_x.inc(SY.dma_start(masks[:, GSZ * ms:GSZ * ms + GSZ], masks_in[a]), 16)
        mask_ld[ms] = s_x.n
        W_qkv = s_s.n
        for pair in range(NPAIR):
            for par in range(2):
                head = 2 * pair + par
                qdst = qe if par == 0 else qo
                W(VE, s_s, W_qkv)
                if par == 0:
                    s_v.inc(VE.tensor_copy(qdst[0:64, :],
                                           Qs[pair][0:64, 128 * qslot:128 * qslot + 128]), 1)
                else:
                    s_v.inc(VE.tensor_copy(qdst[64:128, :],
                                           Qs[pair][64:128, 128 * qslot:128 * qslot + 128]), 1)
                W(TE, s_v, s_v.n)
                puse(P_sc)
                for w in range(3):
                    kt = a + w
                    ii = MM(P_sc[:, 128 * w:128 * w + 128], qdst[:],
                            Ks[pair][:, 128 * (kt % NKV):128 * (kt % NKV) + 128],
                            start=True, stop=True)
                s_mt.inc(ii, 1)
                W(VE, s_mt, s_mt.n)
                W(VE, s_s, s_s.n)
                W(VE, s_x, mask_ld[ms])
                s_v.inc(VE.tensor_add(scs[:], P_sc[:, 0:GSZ],
                                      masks[:, GSZ * ms:GSZ * ms + GSZ]), 1)
                pdone(P_sc, s_v)
                s_v.inc(VE.reduce_max(mxv[:], scs[:], axis=AX), 1)
                VE.drain()
                s_v.inc(VE.tensor_max(mxv[:], mxv[:],
                                      l0buf[:, 16 * lt + head:16 * lt + head + 1]), 1)
                VE.drain()
                s_v.inc(VE.tensor_scalar(ebias[:], mxv[:], -SCALE, None, A.mult), 1)
                VE.drain()
                W(SE, s_v, s_v.n)
                s_s.inc(SE.activation(pbf[:], scs[:], AF.Exp, bias=ebias[:],
                                      scale=SCALE, accum_out=ssum[:]), 1)
                s_s.inc(SE.activation(p0tm[:, head:head + 1],
                                      l0buf[:, 16 * lt + head:16 * lt + head + 1],
                                      AF.Exp, bias=ebias[:], scale=SCALE), 1)
                W(VE, s_s, s_s.n)
                s_v.inc(VE.tensor_add(ssum[:], ssum[:], p0tm[:, head:head + 1]), 1)
                VE.drain()
                s_v.inc(VE.reciprocal(rcpv[:], ssum[:]), 1)
                VE.drain()
                VE.tensor_scalar(pbf[:], pbf[:], rcpv[:], None, A.mult)
                s_v.inc(VE.tensor_scalar(p0tm[:, head:head + 1], p0tm[:, head:head + 1],
                                         rcpv[:], None, A.mult), 1)
                if a == 0:
                    VE.tensor_copy(m0keep[:, head:head + 1], mxv[:])
                    s_v.inc(VE.tensor_copy(s0keep[:, head:head + 1], ssum[:]), 1)
                W(TE, s_v, s_v.n)
                for w in range(3):
                    puse(P_tr2)
                    ii = TE.transpose(P_tr2[:, 128 * w:128 * w + 128],
                                      pbf[:, 128 * w:128 * w + 128], ident[:])
                s_mt.inc(ii, 1)
                W(VE, s_mt, s_mt.n)
                s_v.inc(VE.tensor_copy(pts[:], P_tr2[:, 0:GSZ]), 1)
                pdone(P_tr2, s_v)
                VE.drain()
                if a == 15 and head == 5 and "pdbg" in taps:
                    W(SY, s_v, s_v.n)
                    s_st.inc(SY.dma_start(tap("ptsd", [128, GSZ], BF16), pts[:]), 16)
                    s_st.inc(SY.dma_start(tap("pdbg", [128, GSZ], BF16), pbf[:]), 16)
                    s_st.inc(SY.dma_start(tap("pdbg0", [128, 1], BF16),
                                          p0tm[:, head:head + 1]), 16)
                    s_st.inc(SY.dma_start(tap("pdbgs", [128, GSZ]), scs[:]), 16)
                W(TE, s_v, s_v.n)
                pcols = slice(128 * par, 128 * par + 128)
                if par == 0:
                    puse(P_av)
                for w in range(3):
                    kt = a + w
                    ii = MM(P_av[:, pcols], Vs[kt % NKV][:, 128 * pair:128 * pair + 128],
                            pts[:, 128 * w:128 * w + 128],
                            start=(w == 0), stop=(w == 2))
                s_mt.inc(ii, 1)
            W(VE, s_mt, s_mt.n)
            VE.tensor_copy(out_t[0:64, 128 * pair:128 * pair + 128], P_av[0:64, 0:128])
            s_v.inc(VE.tensor_copy(out_t[64:128, 128 * pair:128 * pair + 128],
                                   P_av[64:128, 128:256]), 1)
            pdone(P_av, s_v)
        if a == 15 and "avpre" in taps:
            W(SY, s_v, s_v.n)
            s_st.inc(SY.dma_start(tap("avpre", [128, DC * 128], BF16), out_t[:]), 16)
        VE.drain()
        VE.memset(padC[:], 0.0)
        VE.drain()
        s_v.inc(VE.tensor_copy(padC[:, 0:16], p0tm[:]), 1)
        VE.drain()
        W(TE, s_v, s_v.n)
        puse(P_tr2)
        s_mt.inc(TE.transpose(P_tr2[:, 0:128], padC[:], ident[:]), 1)
        W(VE, s_mt, s_mt.n)
        s_v.inc(VE.tensor_copy(p0T[:], P_tr2[:, 0:128]), 1)
        pdone(P_tr2, s_v)
        VE.drain()
        W(TE, s_v, s_v.n)
        for pair in range(DC):
            puse(P_x)
            s_mt.inc(MM(P_x[:, 0:128], v0hm[pair][:], p0T[:],
                        start=True, stop=True), 1)
            W(VE, s_mt, s_mt.n)
            s_v.inc(VE.tensor_add(out_t[:, 128 * pair:128 * pair + 128],
                                  out_t[:, 128 * pair:128 * pair + 128],
                                  P_x[:, 0:128]), 1)
            pdone(P_x, s_v)
        qkv_guard[qslot] = (s_v, s_v.n)
        mask_guard[ms] = s_v.n

    def oproj_tile(a):
        lt = a + 1
        src = attn0 if a == 0 else attnb[a % 2]
        xs = a % 2
        W(TE, s_v, s_v.n)
        for half in range(2):
            pt = P_o if half == 0 else P_x
            puse(pt)
            for c in range(DC):
                ii = MM(pt[:, 0:512], src[:, 128 * c:128 * c + 128],
                        wfm["wo"][c][:, 512 * half:512 * half + 512],
                        start=(c == 0), stop=(c == DC - 1))
            s_mt.inc(ii, 1)
        W(VE, s_mt, s_mt.n)
        if xs in x2_spill:
            W(VE, s_sp, x2_spill[xs])
        for half in range(2):
            pt = P_o if half == 0 else P_x
            cols = slice(512 * half, 512 * half + 512)
            VE.tensor_mul(h2f[:, cols], pt[:, 0:512], gaB[:, cols])
            s_v.inc(VE.tensor_add(x2stage[xs][:, cols], h2f[:, cols],
                                  xbf[:, D * (lt % 6) + 512 * half:
                                      D * (lt % 6) + 512 * half + 512]), 1)
            pdone(pt, s_v)
        X2 = s_v.n
        W(SY, s_v, X2)
        s_sp.inc(SY.dma_start(x2s[128 * a:128 * a + 128, :], x2stage[xs][:]), 16)
        x2_spill[xs] = s_sp.n
        W(VE, s_s, s_s.n)
        VE.bn_stats(mvst[:, 0:6], x2stage[xs][:, 0:512])
        VE.bn_stats(mvst[:, 6:12], x2stage[xs][:, 512:1024])
        VE.drain()
        VE.bn_aggr(mva[:], mvst[:, 0:12])
        VE.drain()
        VE.tensor_scalar(vtmp[:], mva[:, 1:2], EPS, None, A.add)
        VE.drain()
        s_v.inc(VE.reciprocal(rstd[:], vtmp[:]), 1)
        W(SE, s_v, s_v.n)
        SE.activation(vtmp[:], rstd[:], AF.Ln)
        SE.drain()
        s_s.inc(SE.activation(rstd[:], vtmp[:], AF.Exp, scale=0.5), 1)
        W(VE, s_s, s_s.n)
        s_v.inc(VE.scalar_tensor_tensor(nbias[:], mva[:, 0:1], -1.0, rstd[:],
                                        A.mult, A.mult), 1)
        W(SE, s_v, s_v.n)
        s_s.inc(SE.activation(h2f[:], x2stage[xs][:], AF.Identity,
                              bias=nbias[:], scale=rstd[:]), 1)
        x2_guard[xs] = (s_s, s_s.n)
        W(VE, s_s, s_s.n)
        VE.tensor_mul(h2f[:], h2f[:], cb1B[:])
        s_v.inc(VE.tensor_add(h2f[:], h2f[:], sbB[:]), 1)
        quant_fp4(h2f[:], D // 16, h2qb[:])
        W(TE, s_v, s_v.n)
        puse(P_tr)
        for c in range(DC):
            ii = TE.transpose(P_tr[:, 128 * c:128 * c + 128],
                              h2qb[:, 128 * c:128 * c + 128], ident[:])
        s_mt.inc(ii, 1)
        W(VE, s_mt, s_mt.n)
        if "h2tt" in x2_guard:
            W(VE, s_sp, x2_guard["h2tt"])
        s_v.inc(VE.tensor_copy(h2qtt[:], P_tr[:, 0:1024]), 1)
        pdone(P_tr, s_v)
        W(SY, s_v, s_v.n)
        for c in range(DC):
            s_sp.inc(SY.dma_start(h2qT_h[128 * c:128 * c + 128, 128 * a:128 * a + 128],
                                  h2qtt[:, 128 * c:128 * c + 128]), 16)
        x2_guard["h2tt"] = s_sp.n

    NLTOT = NTL + 3 * FARG
    load_xtile(0)
    load_xtile(1)
    for g in range(NG):
        for w in range(3):
            lt = 3 * g + w
            ln_tile(lt, g, w, own=(1 <= lt <= 16))
            if lt + 2 < NLTOT:
                load_xtile(lt + 2)
        qkv_group(g)
        if g == 0 and "qs_g0" in taps:
            dqg = tap("qs_g0", [DC, 128, NKV * 128], BF16)
            W(SY, s_s, s_s.n)
            W(SY, s_v, s_v.n)
            for c in range(DC):
                s_st.inc(SY.dma_start(dqg[c][:, 0:384], Qs[c][:, 0:384]), 16)
        if g >= 1:
            far_group(g)
            if g == 1 and "l0ch_g1" in taps:
                W(SY, s_v, s_v.n)
                s_st.inc(SY.dma_start(tap("l0ch_g1", [16, GSZ]), l0ch[0:16, :]), 16)
                s_st.inc(SY.dma_start(tap("c0c", [128, 1]), c0col[:]), 16)
        if stage >= 3:
            alist = [0] if g == 0 else list(range(3 * g - 2, 3 * g + 1))
            for a in alist:
                attend(a)
                if a >= 1:
                    oproj_tile(a)
            ffnw_some(5 if stage >= 5 else 0)
    for fg in range(FARG):
        g = NG + fg
        for w in range(3):
            lt = 3 * g + w
            ln_tile(lt, g, w, own=False)
            if lt + 2 < NLTOT:
                load_xtile(lt + 2)
        far_group(g)
        if fg == FARG - 1 and "l0ch_gl" in taps:
            W(SY, s_v, s_v.n)
            s_st.inc(SY.dma_start(tap("l0ch_gl", [16, GSZ]), l0ch[0:16, :]), 16)
        if fg == FARG - 1 and "xnT_gl" in taps:
            W(SY, s_s, s_s.n)
            s_st.inc(SY.dma_start(tap("xnT_gl", [128, DC * GSZ], BF16),
                                  xnT[:, 0:DC * GSZ]), 16)
        if stage >= 3:
            ffnw_some(5 if stage >= 5 else 0)
    if stage >= 5:
        ffnw_some(len(ffnw_items))

    if "l0buf" in taps:
        W(SY, s_v, s_v.n)
        s_st.inc(SY.dma_start(tap("l0buf", [128, (NTL + 1) * 16]), l0buf[:]), 16)
    if "sacc" in taps:
        W(SY, s_v, s_v.n)
        taps = tuple(taps) + ("m0far", "s0far")
        s_st.inc(SY.dma_start(tap("sacc", [128, D]), sacc[:]), 16)
        s_st.inc(SY.dma_start(tap("m0far", [128, 1]), m0far[:]), 16)
        s_st.inc(SY.dma_start(tap("s0far", [128, 1]), s0far[:]), 16)
    if "xnT" in taps:
        W(SY, s_s, s_s.n)
        s_st.inc(SY.dma_start(tap("xnT", [128, 2 * DC * GSZ], BF16), xnT[:]), 16)
    if "qs" in taps:
        W(SY, s_s, s_s.n)
        dq = tap("qs", [DC, 128, NKV * 128], BF16)
        for c in range(DC):
            s_st.inc(SY.dma_start(dq[c], Qs[c][:]), 16)
    if "ks" in taps:
        dq = tap("ks", [DC, 128, NKV * 128], BF16)
        for c in range(DC):
            s_st.inc(SY.dma_start(dq[c], Ks[c][:]), 16)
    if "vs" in taps:
        dq = tap("vs", [NKV, 128, D], BF16)
        for c in range(NKV):
            s_st.inc(SY.dma_start(dq[c], Vs[c][:]), 16)
    if "attn1" in taps:
        W(SY, s_v, s_v.n)
        s_st.inc(SY.dma_start(tap("attn1", [128, DC * 128], BF16), attnb[1][:]), 16)
    if "x2v" in taps:
        W(SY, s_v, s_v.n)
        W(SY, s_s, s_s.n)
        s_st.inc(SY.dma_start(tap("x2v", [128, D], BF16), x2stage[0][:]), 16)

    if stage <= 3:
        return finalize()

    def row_transpose(col_ap, dst_row_ap):
        VE.memset(padC[:, 0:1], 0.0)
        VE.drain()
        s_v.inc(VE.tensor_copy(padC[:, 0:1], col_ap), 1)
        VE.drain()
        W(TE, s_v, s_v.n)
        puse(P_tr2)
        s_mt.inc(TE.transpose(P_tr2[:, 0:128], padC[:], ident[:]), 1)
        W(VE, s_mt, s_mt.n)
        s_v.inc(VE.tensor_copy(dst_row_ap, P_tr2[0:1, 0:16]), 1)
        pdone(P_tr2, s_v)
        VE.drain()

    row_transpose(m0far[:, 0:1], rows0[0:1, 0:16])
    row_transpose(s0far[:, 0:1], rows0[0:1, 16:32])
    VE.tensor_max(rows0[0:1, 32:48], rows0[0:1, 0:16], m0keep[0:1, 0:16])
    VE.drain()
    VE.tensor_sub(rows0[0:1, 48:64], m0keep[0:1, 0:16], rows0[0:1, 32:48])
    s_v.inc(VE.tensor_sub(rows0[0:1, 64:80], rows0[0:1, 0:16], rows0[0:1, 32:48]), 1)
    VE.drain()
    W(SE, s_v, s_v.n)
    SE.activation(rows0[0:1, 48:64], rows0[0:1, 48:64], AF.Exp, scale=SCALE)
    SE.drain()
    s_s.inc(SE.activation(rows0[0:1, 64:80], rows0[0:1, 64:80], AF.Exp, scale=SCALE), 1)
    W(VE, s_s, s_s.n)
    s_v.inc(VE.tensor_mul(rows0[0:1, 48:64], rows0[0:1, 48:64], s0keep[0:1, 0:16]), 1)
    VE.drain()
    VE.tensor_mul(rows0[0:1, 96:112], rows0[0:1, 64:80], rows0[0:1, 16:32])
    VE.drain()
    VE.tensor_add(rows0[0:1, 80:96], rows0[0:1, 48:64], rows0[0:1, 96:112])
    VE.drain()
    s_v.inc(VE.reciprocal(rows0[0:1, 80:96], rows0[0:1, 80:96]), 1)
    VE.drain()
    s_v.inc(VE.tensor_mul(sacc[0:16, :], sacc[0:16, :], ca1B[0:16, :]), 1)
    VE.drain()
    s_v.inc(VE.tensor_copy(saccb[0:16, :], sacc[0:16, :]), 1)
    VE.drain()
    W(TE, s_v, s_v.n)
    for ic in range(DC):
        puse(P_tr2)
        s_mt.inc(TE.transpose(P_tr2[:, 0:128], saccb[:, 128 * ic:128 * ic + 128],
                              ident[:]), 1)
        W(VE, s_mt, s_mt.n)
        s_v.inc(VE.tensor_copy(sTf[:, 16 * ic:16 * ic + 16], P_tr2[:, 0:16]), 1)
        pdone(P_tr2, s_v)
    VE.drain()
    W(TE, s_v, s_v.n)
    for c in range(DC):
        puse(P_x)
        for ic in range(DC):
            ii = MM(P_x[:, 0:16], wfm["wv"][ic][:, 128 * c:128 * c + 128],
                    sTf[:, 16 * ic:16 * ic + 16], start=(ic == 0), stop=(ic == DC - 1))
        s_mt.inc(ii, 1)
        W(VE, s_mt, s_mt.n)
        VE.tensor_copy(q0far[0:64, c:c + 1], P_x[0:64, 2 * c:2 * c + 1])
        s_v.inc(VE.tensor_copy(q0far[64:128, c:c + 1],
                               P_x[64:128, 2 * c + 1:2 * c + 2]), 1)
        pdone(P_x, s_v)
    VE.drain()

    def colize(row_ap, dstc):
        VE.memset(padC[0:1, :], 0.0)
        VE.drain()
        s_v.inc(VE.tensor_copy(padC[0:1, 0:16], row_ap), 1)
        VE.drain()
        W(TE, s_v, s_v.n)
        puse(P_tr2)
        s_mt.inc(TE.transpose(P_tr2[:, 0:128], padC[:], ident[:]), 1)
        W(VE, s_mt, s_mt.n)
        s_v.inc(VE.tensor_copy(ebf[0:16, 0:1], P_tr2[0:16, 0:1]), 1)
        pdone(P_tr2, s_v)
        VE.drain()
        s_v.inc(VE.tensor_tensor(ebf2[0:16, :], parmask[0:16, :],
                                 ebf[0:16, 0:1].broadcast_to([16, 8]), A.mult), 1)
        VE.drain()
        W(TE, s_v, s_v.n)
        puse(P_x)
        s_mt.inc(MM(P_x[:, 0:8], epar[:], ebf2[:, 0:8], start=True, stop=True), 1)
        W(VE, s_mt, s_mt.n)
        s_v.inc(VE.tensor_copy(dstc[:], P_x[:, 0:8]), 1)
        pdone(P_x, s_v)
        VE.drain()

    colize(rows0[0:1, 16:32], colR)
    VE.tensor_mul(ctmp[:], bvv[:], colR[:])
    VE.drain()
    s_v.inc(VE.tensor_add(q0far[:], q0far[:], ctmp[:]), 1)
    VE.drain()
    colize(rows0[0:1, 48:64], colA)
    colize(rows0[0:1, 64:80], colF)
    colize(rows0[0:1, 80:96], colR)
    for c in range(DC):
        col = attn0[:, 128 * c:128 * c + 1]
        VE.tensor_mul(coltmp[:], col, colA[:, c:c + 1])
        VE.drain()
        VE.tensor_mul(vtmp[:], q0far[:, c:c + 1], colF[:, c:c + 1])
        VE.drain()
        VE.tensor_add(coltmp[:], coltmp[:], vtmp[:])
        VE.drain()
        VE.tensor_mul(coltmp[:], coltmp[:], colR[:, c:c + 1])
        VE.drain()
        VE.tensor_sub(coltmp[:], coltmp[:], col)
        VE.drain()
        VE.tensor_scalar(coltmp[:], coltmp[:], qg[:, 0:1], None, A.mult)
        VE.drain()
        s_v.inc(VE.tensor_add(col, col, coltmp[:]), 1)
        VE.drain()
    if "attn0d" in taps:
        W(SY, s_v, s_v.n)
        s_st.inc(SY.dma_start(tap("attn0d", [128, DC * 128], BF16), attn0[:]), 16)
    if stage >= 3:
        for sem, val in xst_guard.get(1, ()):
            W(SY, sem, val)
        s_x.inc(SY.dma_start(xst[1][:], x_loc[128:256, :]), 16)
        W(VE, s_x, s_x.n)
        s_v.inc(VE.tensor_copy(xbf[:, D * 1:D * 2], xst[1][:]), 1)
        VE.drain()
        oproj_tile(0)


    if stage == 4:
        return finalize()

    preload_set("silu_and_others")
    x2ld = {}
    wbuf_guard = {}
    ost_guard = {}

    def p3_load_wchunk(C, buf):
        off, wdt = ICH[C]
        if buf in wbuf_guard:
            W(SY, s_mt, wbuf_guard[buf])
        for c in range(DC):
            s_w2.inc(SY.dma_start(wgC[c][:, 512 * buf:512 * buf + wdt],
                                  wgq_h[128 * c:128 * c + 128, off:off + wdt]), 16)
            s_w2.inc(SY.dma_start(wuC[c][:, 512 * buf:512 * buf + wdt],
                                  wuq_h[128 * c:128 * c + 128, off:off + wdt]), 16)
        for i in range((wdt + 127) // 128):
            s_w2.inc(SY.dma_start(wdCb[2 * i + buf][:, 0:D],
                                  wdq_h[off + 128 * i:off + 128 * i + 128, :]), 16)

    for H in range(2):
        if H == 0:
            W(SY, s_v, s_v.n)
            W(SY, s_s, s_s.n)
            W(SY, s_mt, s_mt.n)
            W(SY, s_sp, s_sp.n)
        else:
            W(SY, s_mt, s_mt.n)
        for c in range(DC):
            s_w2.inc(SY.dma_start(h2qTb[c][:, 0:1024],
                                  h2qT_h[128 * c:128 * c + 128,
                                         1024 * H:1024 * H + 1024]), 16)
        H2L = s_w2.n
        for i8 in range(8):
            if H == 0 and i8 == 0:
                W(VE, s_sp, s_sp.n)
            if H == 1:
                W(VE, s_st, s_st.n)
            s_v.inc(VE.memset(oacc[i8][:], 0.0), 1)
        for C in range(6):
            off, wdt = ICH[C]
            nit = (wdt + 127) // 128
            buf = C % 2
            if H == 0 and C == 0:
                W(SY, s_sp, s_sp.n)
                p3_load_wchunk(0, 0)
            W_CHUNK = s_w2.n
            nC = C + 1
            if nC < 6:
                p3_load_wchunk(nC, nC % 2)
            elif H == 0:
                p3_load_wchunk(0, 0)
            for t8 in range(8):
                t = 8 * H + t8
                W(TE, s_w2, W_CHUNK)
                W(TE, s_w2, H2L)
                W(TE, s_v, s_v.n)
                for which, pt, wC in ((0, P_mm, wgC), (1, P_mm2, wuC)):
                    puse(pt)
                    for c in range(DC):
                        ii = MM(pt[:, 0:wdt], h2qTb[c][:, 128 * t8:128 * t8 + 128],
                                wC[c][:, 512 * buf:512 * buf + wdt],
                                start=(c == 0), stop=(c == DC - 1))
                    s_mt.inc(ii, 1)
                W(SE, s_mt, s_mt.n)
                s_s.inc(SE.activation(gsil[:, 0:wdt], P_mm[:, 0:wdt], AF.Silu), 1)
                pdone(P_mm, s_s)
                W(VE, s_s, s_s.n)
                s_v.inc(VE.tensor_mul(hmidf := h2f[:, 0:wdt], P_mm2[:, 0:wdt],
                                      gsil[:, 0:wdt]), 1)
                pdone(P_mm2, s_v)
                quant_fp4(h2f[:, 0:wdt], wdt // 16, hmq[:, 0:wdt])
                W(TE, s_v, s_v.n)
                puse(P_tr)
                for i in range(nit):
                    ii = TE.transpose(P_tr[:, 128 * i:128 * i + 128],
                                      hmq[:, 128 * i:128 * i + 128], ident[:])
                s_mt.inc(ii, 1)
                W(VE, s_mt, s_mt.n)
                s_v.inc(VE.tensor_copy(hmqT[:, 0:wdt], P_tr[:, 0:wdt]), 1)
                pdone(P_tr, s_v)
                W(TE, s_v, s_v.n)
                for half in range(2):
                    pt = P_o if half == 0 else P_x
                    puse(pt)
                    for i in range(nit):
                        ii = MM(pt[:, 0:512], hmqT[:, 128 * i:128 * i + 128],
                                wdCb[2 * i + buf][:, 512 * half:512 * half + 512],
                                start=(i == 0), stop=(i == nit - 1))
                    s_mt.inc(ii, 1)
                    W(VE, s_mt, s_mt.n)
                    s_v.inc(VE.tensor_add(oacc[t8][:, 512 * half:512 * half + 512],
                                          oacc[t8][:, 512 * half:512 * half + 512],
                                          pt[:, 0:512]), 1)
                    pdone(pt, s_v)
            wbuf_guard[buf] = s_mt.n
        for t8 in range(8):
            t = 8 * H + t8
            slot = t8 % 2
            if slot in x2ld:
                W(SY, s_v, x2ld[slot])
            else:
                W(SY, s_v, s_v.n)
                W(SY, s_s, s_s.n)
            s_x.inc(SY.dma_start(x2stage[slot][:], x2s[128 * t:128 * t + 128, :]), 16)
            W(VE, s_x, s_x.n)
            VE.tensor_mul(oacc[t8][:], oacc[t8][:], gbB[:])
            s_v.inc(VE.tensor_add(oacc[t8][:], oacc[t8][:], x2stage[slot][:]), 1)
            x2ld[slot] = s_v.n
            W(SY, s_v, s_v.n)
            s_st.inc(SY.dma_start(out_hbm[128 * t:128 * t + 128, :], oacc[t8][:]), 16)

    return finalize()


def _build_masks(half):
    qoff = 2048 * half
    m = np.full((NT, 128, GSZ), NEG, dtype=np.float32)
    for a in range(NT):
        q = qoff + 128 * a + np.arange(128)[:, None]
        k = qoff - 128 + 128 * a + np.arange(GSZ)[None, :]
        att = (np.abs(q - k) <= 128) & (k >= 0) & (k < T) & (k != 0)
        if half == 0 and a == 0:
            att[0, :] = (k[0] >= 1) & (k[0] < T)
        m[a][att] = 0.0
    return m


def _host_inputs(inputs):
    x = np.asarray(inputs["L0_attn_x"], dtype=np.float32)
    c = np.asarray(inputs["L0_c"], dtype=np.float32)
    import ml_dtypes
    bf = ml_dtypes.bfloat16
    ident = np.eye(128, dtype=bf)
    e0 = np.zeros((128, 128), dtype=bf); e0[0, :] = 1
    hm = np.zeros((128, 2), dtype=bf); hm[0:64, 0] = 1; hm[64:128, 1] = 1
    pmsk = np.zeros((16, 8), dtype=bf)
    for h in range(16):
        pmsk[h, h // 2] = 1
    ep = np.zeros((128, 128), dtype=bf)
    for p in range(128):
        for h in range(16):
            if (p >= 64) == (h % 2 == 1):
                ep[h, p] = 1
    shared = {
        "wmod": np.ascontiguousarray(inputs["L0_wmod"], dtype=np.float32),
        "wq": np.ascontiguousarray(inputs["L0_attn_Wq"], dtype=np.float32),
        "wk": np.ascontiguousarray(inputs["L0_attn_Wk"], dtype=np.float32),
        "wv": np.ascontiguousarray(inputs["L0_attn_Wv"], dtype=np.float32),
        "wo": np.ascontiguousarray(inputs["L0_attn_Wo"], dtype=np.float32),
        "wg": np.ascontiguousarray(inputs["L0_ffn_w_gate"], dtype=np.float32),
        "wu": np.ascontiguousarray(inputs["L0_ffn_w_up"], dtype=np.float32),
        "wd": np.ascontiguousarray(inputs["L0_ffn_w_down"], dtype=np.float32),
        "ident": ident, "e0": e0, "halfmask": hm, "parmask": pmsk, "epar": ep,
    }
    masks_by_half = [_build_masks(0).astype(bf), _build_masks(1).astype(bf)]
    in_maps = []
    for core in range(8):
        b, half = core >> 1, core & 1
        qoff = 2048 * half
        xl = np.zeros((T_LOC, D), dtype=np.float32)
        lo, hi = qoff - 128, qoff + 2176
        slo, shi = max(lo, 0), min(hi, T)
        xl[slo - lo:shi - lo] = x[b, slo:shi]
        xf = (np.ascontiguousarray(x[b, 2176:4096]) if half == 0
              else np.zeros((T_FAR, D), dtype=np.float32))
        qgv = np.full((128, 1), 1.0 if half == 0 else 0.0, dtype=np.float32)
        m = {"x_loc": xl, "x_far": xf,
             "x0": np.ascontiguousarray(x[b, 0:1]),
             "c_fm": np.ascontiguousarray(c[b].reshape(8, 128).T),
             "masks": masks_by_half[half], "qg": qgv}
        m.update(shared)
        in_maps.append(m)
    return in_maps


def kernel(**inputs):
    in_maps = _host_inputs(inputs)
    nc, _ = build_nc(stage=5)
    res = run_bass_kernel_spmd(nc, in_maps, core_ids=list(range(8)))
    out = np.zeros((B, T, D), dtype=np.float32)
    for core in range(8):
        b, half = core >> 1, core & 1
        out[b, 2048 * half:2048 * half + 2048] = res.results[core]["out"]
    return out
